# version 10 (final)
# baseline (speedup 1.0000x reference)
import numpy as np
import concourse.bass as bass
import concourse.mybir as mybir
from concourse.bass_utils import run_bass_kernel_spmd

F32 = mybir.dt.float32
BF16 = mybir.dt.bfloat16
U8 = mybir.dt.uint8
AF = mybir.ActivationFunctionType
ALU = mybir.AluOpType

D = 1024
S = 4096
T = 512
NT = S // T
KC = D // 128
FH = 4 * D
FKC = FH // 128
EPS = 1e-5
NCORES = 8

RF_RSTD = 0
RF_RR = 512
RF_MU = 1024
RF_R1 = 1536
RF_R2 = 2048
RF_R3 = 2560
RF_ONEF = 3072
ROWSF_ELEMS = 3200
RF_ABROW = RF_MU
RF_C2F = RF_R2
RB_NEGMU = 0
RB_INVRR = 512
RB_NRI = 1024
RB_RSQ = 1536
RB_RSV = 2560
RB_RSG = 3584
RB_RSA = 4608
RB_C2 = 5632
RB_ONEB = 6656
ROWSB_ELEMS = 6784
NRING = 10
PF = 6


class P:

    SEMS = (["T", "V", "A", "SM", "GX", "XL", "OU"] +
            ["GIN%d" % i for i in range(10)] +
            ["RGS%d" % i for i in range(10)] +
            ["PKS%d" % i for i in range(10)])

    def __init__(self):
        self.ops = []
        self.cnt = {n: 0 for n in self.SEMS}

    def emit(self, eng, fn, deps=(), tok=True, d=1, sem=None):
        sem = sem or eng
        waits = {}
        for t in deps:
            if t is None:
                continue
            te, ts, tv, ta = t
            if te == eng and not ta:
                continue
            if waits.get(ts, 0) < tv:
                waits[ts] = tv
        token = None
        incs = []
        if tok:
            self.cnt[sem] += d
            incs = [(sem, d)]
            token = (eng, sem, self.cnt[sem], d == 16)
        self.ops.append((eng, fn, waits, incs))
        return token


def build_nc():
    nc = bass.Bass("TRN2", target_bir_lowering=False,
                   detect_race_conditions=False)

    xT = nc.dram_tensor("xT", [D, S], F32, kind="ExternalInput").ap()
    wqT = nc.dram_tensor("wqT", [D, D], F32, kind="ExternalInput").ap()
    wvT = nc.dram_tensor("wvT", [D, D], F32, kind="ExternalInput").ap()
    wgT = nc.dram_tensor("wgT", [D, D], F32, kind="ExternalInput").ap()
    woT = nc.dram_tensor("woT", [D, D], F32, kind="ExternalInput").ap()
    awT = nc.dram_tensor("awT", [D, D], F32, kind="ExternalInput").ap()
    nwT = nc.dram_tensor("nwT", [D, D], F32, kind="ExternalInput").ap()
    w1p = nc.dram_tensor("w1p", [2, D, FH], F32, kind="ExternalInput").ap()
    w2p = nc.dram_tensor("w2p", [2, FH, D], F32, kind="ExternalInput").ap()
    b1c_d = nc.dram_tensor("b1c", [128, 64], F32, kind="ExternalInput").ap()
    b2c_d = nc.dram_tensor("b2c", [128, 16], F32, kind="ExternalInput").ap()
    abrow_d = nc.dram_tensor("abrow", [1, D], F32, kind="ExternalInput").ap()
    nbc_d = nc.dram_tensor("nbc", [128, 8], F32, kind="ExternalInput").ap()
    g3c_d = nc.dram_tensor("g3c", [128, 8], F32, kind="ExternalInput").ap()
    b3c_d = nc.dram_tensor("b3c", [128, 8], F32, kind="ExternalInput").ap()
    g2c_d = nc.dram_tensor("g2c", [128, 8], F32, kind="ExternalInput").ap()
    b2vc_d = nc.dram_tensor("b2vc", [128, 8], F32, kind="ExternalInput").ap()
    lam_d = nc.dram_tensor("lamh", [128, 8], F32, kind="ExternalInput").ap()
    outT = nc.dram_tensor("outT", [D, S], F32, kind="ExternalOutput").ap()
    pw = {}
    for nm in ["aw", "wq", "wv", "wg", "wo", "nw"]:
        pw[nm] = nc.dram_tensor(nm + "b", [D, D], BF16, kind="Internal").ap()
    w1b_d = nc.dram_tensor("w1b", [2, FKC, 128, 1024], BF16,
                           kind="Internal").ap()
    w2b_d = nc.dram_tensor("w2b", [2, FH, D], BF16, kind="Internal").ap()
    consts_np = np.zeros((128, 8), dtype=np.float32)
    consts_np[:, 0] = EPS
    consts_np[:, 1] = -0.5
    consts_np[:, 2] = 0.5
    consts_np[:, 3] = 0.25
    consts_d = nc.inline_tensor(consts_np, name="constsrom").ap()

    prog = P()
    E = prog.emit

    from contextlib import ExitStack
    with ExitStack() as stack:
        en = stack.enter_context
        rows_f = en(nc.sbuf_tensor("rowsf", [1, ROWSF_ELEMS], F32))
        rows_b = en(nc.sbuf_tensor("rowsb", [1, ROWSB_ELEMS], BF16))
        scrv_t = en(nc.sbuf_tensor("scrv", [128, 1024], F32))
        scra_t = en(nc.sbuf_tensor("scra", [128, 2048], F32))
        b1c = en(nc.sbuf_tensor("b1cs", [128, 64], F32))
        b2c = en(nc.sbuf_tensor("b2cs", [128, 16], F32))
        nbc = en(nc.sbuf_tensor("nbcs", [128, 8], F32))
        g3c = en(nc.sbuf_tensor("g3cs", [128, 8], F32))
        b3c = en(nc.sbuf_tensor("b3cs", [128, 8], F32))
        g2c = en(nc.sbuf_tensor("g2cs", [128, 8], F32))
        b2vc = en(nc.sbuf_tensor("b2vcs", [128, 8], F32))
        g2cb = en(nc.sbuf_tensor("g2cbs", [128, 8], BF16))
        b2vcb = en(nc.sbuf_tensor("b2vcbs", [128, 8], BF16))
        lam = en(nc.sbuf_tensor("lams", [128, 8], F32))
        carry = en(nc.sbuf_tensor("carrys", [128, 8], F32))
        cons = en(nc.sbuf_tensor("constss", [128, 8], F32))
        onescol = en(nc.sbuf_tensor("onescol", [128, 1], BF16))
        hbf_t = en(nc.sbuf_tensor("hbft", [128, 16384], BF16))
        xt_t = en(nc.sbuf_tensor("xtt", [128, 4096], F32))
        stt_t = en(nc.sbuf_tensor("stts", [128, 4096], F32))
        qt_t = en(nc.sbuf_tensor("qtt", [128, 4096], F32))
        vt_t = en(nc.sbuf_tensor("vtt", [128, 4096], F32))
        xbf_t = en(nc.sbuf_tensor("xbft", [128, 4096], BF16))
        ybf_t = en(nc.sbuf_tensor("ybft", [128, 4096], BF16))
        xg_t = en(nc.sbuf_tensor("xgt", [128, 4096], BF16))
        xabf_t = en(nc.sbuf_tensor("xabft", [128, 4096], BF16))
        eobf_t = en(nc.sbuf_tensor("eobft", [128, 4096], BF16))
        ring_t = en(nc.sbuf_tensor("ringt", [128, NRING * 1024], BF16))
        bc_t = en(nc.sbuf_tensor("bct", [128, 2048], F32))
        PS = en(nc.psum_tensor("PS", [128, 4096], F32))
        sems = {nm: en(nc.semaphore("sem" + nm)) for nm in P.SEMS}
        block = en(nc.Block())

        def row(off, n, dt=F32):
            t = rows_f if dt == F32 else rows_b
            return t[:, off:off + n]

        def scrv(i):
            return scrv_t[:, i * 512:(i + 1) * 512]

        def scra(i):
            return scra_t[:, i * 512:(i + 1) * 512]

        def bank(b):
            return PS[:, 512 * b:512 * (b + 1)]

        def bankrow(b):
            return PS[0:1, 512 * b:512 * (b + 1)]

        cEPS = cons[0:1, 0:1]
        cNH = cons[0:1, 1:2]
        cPH = cons[0:1, 2:3]
        cQT = cons[0:1, 3:4]
        cPH128 = cons[:, 2:3]
        cQT128 = cons[:, 3:4]

        hbf = hbf_t[:, :]
        xt = xt_t[:, :]
        stt = stt_t[:, :]
        qt = qt_t[:, :]
        vt = vt_t[:, :]
        xbf = xbf_t[:, :]
        ybf = ybf_t[:, :]
        xg = xg_t[:, :]
        xabf = xabf_t[:, :]
        eobf = eobf_t[:, :]
        ring = [ring_t[:, 1024 * i:1024 * (i + 1)] for i in range(NRING)]
        bc_rstd = bc_t[:, 0:512]
        bc_rr = bc_t[:, 512:1024]
        bc_m3 = bc_t[:, 1024:1536]
        bc_r3 = bc_t[:, 1536:2048]

        negmu = row(RB_NEGMU, T, BF16)
        invrr = row(RB_INVRR, T, BF16)
        nri = row(RB_NRI, T, BF16)
        r_rstd = row(RF_RSTD, T)
        r_rr = row(RF_RR, T)
        r_mu = row(RF_MU, T)
        r_1 = row(RF_R1, T)
        r_2 = row(RF_R2, T)
        r_3 = row(RF_R3, T)
        rsq = row(RB_RSQ, 1024, BF16)
        rsv = row(RB_RSV, 1024, BF16)
        rsg = row(RB_RSG, 1024, BF16)
        rsA = row(RB_RSA, 1024, BF16)
        c2b = row(RB_C2, 1024, BF16)
        onef = row(RF_ONEF, 128)
        oneb = row(RB_ONEB, 128, BF16)
        abrow = row(RF_ABROW, 1024)
        c2f = row(RF_C2F, 1024)

        def dm(dst, src):
            return lambda eb: eb.dma_start(out=dst, in_=src)

        def csl(c):
            return slice(512 * c, 512 * (c + 1))

        t_small = None
        pre_x = {}
        for dst, src in [(b1c[:], b1c_d), (b2c[:], b2c_d), (nbc[:], nbc_d),
                         (g3c[:], g3c_d), (b3c[:], b3c_d), (g2c[:], g2c_d),
                         (b2vc[:], b2vc_d), (lam[:], lam_d),
                         (abrow, abrow_d), (cons[:], consts_d)]:
            t_small = E("S", dm(dst, src), d=16, sem="SM")

        E("V", lambda eb: eb.memset(onescol[:], 1.0))
        E("V", lambda eb: eb.memset(onef, 1.0))
        t_ones = E("V", lambda eb: eb.memset(oneb, 1.0))
        t_carry0 = E("V", lambda eb: eb.memset(carry[:], 0.0))
        t_g2b = E("V", lambda eb: eb.tensor_copy(g2cb[:], g2c[:]),
                  deps=[t_small])
        t_b2b = E("V", lambda eb: eb.tensor_copy(b2vcb[:], b2vc[:]),
                  deps=[t_small])

        def xload_f32(t0):
            return lambda eb: eb.dma_start(
                out=xt.rearrange("p (c t) -> p c t", c=KC),
                in_=xT[:, t0:t0 + T].rearrange("(c p) t -> p c t", p=128))

        def xload_bf(t0):
            return lambda eb: eb.dma_start(
                out=xbf.rearrange("p (c t) -> p c t", c=KC),
                in_=xT[:, t0:t0 + T].rearrange("(c p) t -> p c t", p=128))

        pre_x[0] = (E("G", xload_bf(0), d=16, sem="GX"),
                    E("S", xload_f32(0), d=16, sem="XL"))

        def mk_nonmoe(fp, pk):
            return (lambda k: fp[128 * k:128 * (k + 1), :],
                    lambda k: pk[128 * k:128 * (k + 1), :])

        def mk_w1(e):
            def f32s(j):
                k, g = j % KC, j // KC
                return w1p[e, 128 * k:128 * (k + 1),
                           1024 * g:1024 * (g + 1)]
            def pks(j):
                return w1b_d[e, j]
            return (f32s, pks)

        def mk_w2(e):
            return (lambda k: w2p[e, 128 * k:128 * (k + 1), :],
                    lambda k: w2b_d[e, 128 * k:128 * (k + 1), :])

        SRC = {
            "aw": mk_nonmoe(awT, pw["aw"]),
            "wq": mk_nonmoe(wqT, pw["wq"]),
            "wv": mk_nonmoe(wvT, pw["wv"]),
            "wg": mk_nonmoe(wgT, pw["wg"]),
            "wo": mk_nonmoe(woT, pw["wo"]),
            "nw": mk_nonmoe(nwT, pw["nw"]),
            "w10": mk_w1(0), "w11": mk_w1(1),
            "w20": mk_w2(0), "w21": mk_w2(1),
        }
        parked = set()
        park_last = {}
        ring_state = {"idx": 0, "free": [None] * NRING}

        def ringload(name, k):
            ri = ring_state["idx"] % NRING
            ring_state["idx"] += 1
            fr = ring_state["free"][ri]
            fr = fr if isinstance(fr, list) else [fr]
            assert name in parked, name
            t_ld = E("S", dm(ring[ri], pkf_of[name](k)),
                     deps=fr + [park_last.get(name)], d=16,
                     sem="RGS%d" % ri)
            return ri, t_ld, None

        pkf_of = {nm: SRC[nm][1] for nm in SRC}

        def park_stream(name, nchunks):
            f32f, pkf = SRC[name]
            pks = []
            for j in range(nchunks):
                ri = ring_state["idx"] % NRING
                ring_state["idx"] += 1
                fr = ring_state["free"][ri]
                fr = fr if isinstance(fr, list) else [fr]
                t_ld = E("G", dm(ring[ri], f32f(j)), deps=fr, d=16,
                         sem="GIN%d" % ri)
                t_pk = E("S", dm(pkf(j), ring[ri]), deps=[t_ld], d=16,
                         sem="PKS%d" % ri)
                ring_state["free"][ri] = [t_pk]
                pks.append(t_pk)
            parked.add(name)
            park_last[name] = pks[-1]

        def park_stream2(name, base, n):
            f32f, pkf = SRC[name]
            pks = []
            for j in range(base, base + n):
                ri = ring_state["idx"] % NRING
                ring_state["idx"] += 1
                fr = ring_state["free"][ri]
                fr = fr if isinstance(fr, list) else [fr]
                t_ld = E("G", dm(ring[ri], f32f(j)), deps=fr, d=16,
                         sem="GIN%d" % ri)
                t_pk = E("S", dm(pkf(j), ring[ri]), deps=[t_ld], d=16,
                         sem="PKS%d" % ri)
                ring_state["free"][ri] = [t_pk]
                pks.append(t_pk)
            park_last[name + "_p%d" % base] = pks[-1]
            if base + n == FKC:
                parked.add(name)
                park_last[name] = pks[-1]

        def mark_parked(name, pks):
            if pks and name not in parked:
                parked.add(name)
                park_last[name] = pks[-1]

        def wide_phase(name, rhs_fn, ebank, base=0, nk=KC, krows=None,
                       krow_deps=(), rhs_deps=None, extra_dep=None):
            pks = []
            lds = []
            for kk in range(min(PF, nk)):
                lds.append(ringload(name, base + kk))
            tok = None
            for k in range(nk):
                if k + PF < nk:
                    lds.append(ringload(name, base + k + PF))
                ri, t_ld, t_pk = lds[k]
                if t_pk is not None:
                    pks.append(t_pk)
                rdep = rhs_deps[k] if rhs_deps else None
                pkfree = [t_pk] if t_pk is not None else []
                if k == 0:
                    for m in range(8):
                        def mm0(eb, ri=ri, m=m):
                            return eb.matmul(
                                bank(m), ring[ri][:, 128 * m:128 * (m + 1)],
                                rhs_fn(0), start=True, stop=False)
                        tok = E("T", mm0, deps=[t_ld, ebank[m], rdep,
                                                extra_dep])
                else:
                    def mmk(eb, ri=ri, k=k):
                        ins = None
                        for m in range(8):
                            ins = eb.matmul(
                                bank(m), ring[ri][:, 128 * m:128 * (m + 1)],
                                rhs_fn(k), start=False,
                                stop=(krows is None and k == nk - 1))
                        return ins
                    tok = E("T", mmk, deps=[t_ld, rdep, extra_dep])
                ring_state["free"][ri] = [tok] + pkfree
            if krows is not None:
                def krf(eb):
                    ins = None
                    for m in range(8):
                        for i, (lh, rh) in enumerate(krows):
                            lhs = (lh[0:1, 128 * m:128 * (m + 1)]
                                   if lh.shape[1] >= 1024 else lh)
                            ins = eb.matmul(
                                bank(m), lhs, rh,
                                start=False, stop=(i == len(krows) - 1))
                    return ins
                tok = E("T", krf, deps=list(krow_deps))
            return tok, pks

        def prologue_rows(name, emit_mm, prev_bank_read):
            pks = []
            toks = []
            f32f, _pkf = SRC[name]
            for half in range(2):
                lds = []
                for kk in range(4):
                    k = 4 * half + kk
                    ri = ring_state["idx"] % NRING
                    ring_state["idx"] += 1
                    fr = ring_state["free"][ri]
                    fr = fr if isinstance(fr, list) else [fr]
                    t_ld = E("G", dm(ring[ri], f32f(k)), deps=fr, d=16,
                             sem="GIN%d" % ri)
                    t_pk = E("S", dm(_pkf(k), ring[ri]), deps=[t_ld],
                             d=16, sem="PKS%d" % ri)
                    lds.append((ri, t_ld, t_pk))
                def mmf(eb, lds=lds, half=half):
                    ins = None
                    for kk in range(4):
                        k = 4 * half + kk
                        ins = emit_mm(eb, k, lds[kk][0], k == 0, k == KC - 1)
                    return ins
                tk = E("T", mmf, deps=[t[1] for t in lds] +
                       [t_ones, prev_bank_read, t_g2b, t_b2b])
                for t in lds:
                    ring_state["free"][t[0]] = ([tk, t[2]]
                                                if t[2] is not None else tk)
                    if t[2] is not None:
                        pks.append(t[2])
                toks.append(tk)
            return toks[-1], pks

        def mk_aw(eb, k, ri, first, last):
            ins = None
            for h in range(2):
                eb.matmul(bankrow(6 + h), g2cb[:, k:k + 1],
                          ring[ri][:, 512 * h:512 * (h + 1)],
                          start=first, stop=last)
                ins = eb.matmul(bankrow(4 + h), b2vcb[:, k:k + 1],
                                ring[ri][:, 512 * h:512 * (h + 1)],
                                start=first, stop=last)
            return ins

        def mk_rs(eb, k, ri, first, last):
            ins = None
            for h in range(2):
                ins = eb.matmul(bankrow(6 + h), onescol[:],
                                ring[ri][:, 512 * h:512 * (h + 1)],
                                start=first, stop=last)
            return ins

        bank_read = t_ones
        rs_out = {}
        t_awmm, aw_pks = prologue_rows("aw", mk_aw, bank_read)
        def ev_rsa(eb):
            eb.copy(rsA[0:1, 0:512], bankrow(6))
            return eb.copy(rsA[0:1, 512:1024], bankrow(7))
        t_rsa = E("A", ev_rsa, deps=[t_awmm])
        def ev_c2(eb):
            eb.tensor_copy(c2f[0:1, 0:512], bankrow(4))
            eb.tensor_copy(c2f[0:1, 512:1024], bankrow(5))
            eb.tensor_add(c2f, c2f, abrow)
            return eb.tensor_copy(c2b, c2f)
        t_c2 = E("V", ev_c2, deps=[t_awmm, t_small])
        mark_parked("aw", aw_pks)
        bank_read = t_rsa
        for nm, dstrow in [("wq", rsq), ("wv", rsv)]:
            tk, pks = prologue_rows(nm, mk_rs, bank_read)
            def evf(eb, dstrow=dstrow):
                eb.copy(dstrow[0:1, 0:512], bankrow(6))
                return eb.copy(dstrow[0:1, 512:1024], bankrow(7))
            bank_read = E("A", evf, deps=[tk])
            rs_out[nm] = bank_read
            mark_parked(nm, pks)

        park_stream("w10", FKC)
        park_stream("w20", FKC)
        for nm, dstrow in [("wg", rsg)]:
            tk, pks = prologue_rows(nm, mk_rs, bank_read)
            def evf(eb, dstrow=dstrow):
                eb.copy(dstrow[0:1, 0:512], bankrow(6))
                return eb.copy(dstrow[0:1, 512:1024], bankrow(7))
            bank_read = E("A", evf, deps=[tk])
            rs_out[nm] = bank_read
            mark_parked(nm, pks)
        park_stream("wo", KC)
        park_stream("nw", KC)

        ebank = {0: t_ones, 1: t_ones, 2: t_ones, 3: t_ones,
                 4: t_c2, 5: t_c2, 6: t_rsa, 7: t_rsa}
        xbf_guard = [None]
        xg_guard = [None]
        ybf_guard = [None]
        xa_guard = [None]
        hbf_guard = [None]
        eo_guard = [None]
        stt_guard = [None]
        xt_guard = [None]
        vt_guard = [None]
        row_guard = [t_c2]
        stats_guard = [t_c2]
        bc_reads = {"rstd": t_rsa, "rr": t_rsa, "m3": t_rsa, "r3": t_rsa}
        prev_scra = [None, None]
        t_out = None
        w1_pks = {}

        for ti in range(NT):
            t0 = ti * T
            eix = ti // 4
            w1nm = "w1" + str(eix)
            w2nm = "w2" + str(eix)

            t_xbf, t_xtl = pre_x[ti]
            if ti < 4:
                park_stream2("w11", 8 * ti, 8)
                park_stream2("w21", 8 * ti, 8)

            def sqf(eb):
                ins = None
                for c in range(KC):
                    ins = eb.activation(xg[:, csl(c)], xbf[:, csl(c)],
                                        AF.Square)
                return ins
            t_sq = E("A", sqf, deps=[t_xbf] + xg_guard)

            def mk_stats(bk, src):
                def mmf(eb, bk=bk, src=src):
                    ins = None
                    for k in range(KC):
                        ins = eb.matmul(bankrow(bk), onescol[:],
                                        src[:, csl(k)],
                                        start=(k == 0), stop=(k == KC - 1))
                    return ins
                return mmf
            t_mS = E("T", mk_stats(2, xbf),
                     deps=[t_xbf, ebank[2], t_ones] + stats_guard)
            t_mQ = E("T", mk_stats(3, xg),
                     deps=[t_sq, ebank[3]] + stats_guard)

            t_mu = E("V", lambda eb: eb.tensor_scalar_mul(
                r_mu, bankrow(2), 1.0 / D), deps=[t_mS] + row_guard)
            t_nmu = E("V", lambda eb: eb.tensor_scalar_mul(negmu, r_mu,
                                                           -1.0))
            E("V", lambda eb: eb.tensor_mul(r_1, r_mu, r_mu), tok=False)
            t_var = E("V", lambda eb: eb.scalar_tensor_tensor(
                r_2, bankrow(3), 1.0 / D, r_1, op0=ALU.mult,
                op1=ALU.subtract), deps=[t_mQ])
            t_ln1 = E("A", lambda eb: eb.activation(r_3, r_2, AF.Ln,
                                                    bias=cEPS),
                      deps=[t_var] + row_guard)
            t_rstd = E("A", lambda eb: eb.activation(
                r_rstd, r_3, AF.Exp, bias=0.0, scale=cNH))
            E("V", lambda eb: eb.tensor_mul(r_1, r_rstd, r_rstd),
              deps=[t_rstd], tok=False)
            t_var2 = E("V", lambda eb: eb.tensor_mul(r_1, r_1, r_2))
            t_ln2 = E("A", lambda eb: eb.activation(r_2, r_1, AF.Ln,
                                                    bias=cEPS),
                      deps=[t_var2])
            t_lnsum = E("V", lambda eb: eb.tensor_add(r_1, r_3, r_2),
                        deps=[t_ln2])
            t_invrr = E("A", lambda eb: eb.activation(
                invrr, r_1, AF.Exp, bias=0.0, scale=cPH), deps=[t_lnsum])
            t_rstd2 = E("A", lambda eb: eb.activation(
                r_2, r_2, AF.Exp, bias=0.0, scale=cNH), deps=[t_lnsum])
            t_rr = E("V", lambda eb: eb.tensor_mul(r_rr, r_rstd, r_2),
                     deps=[t_rstd2])
            E("V", lambda eb: eb.tensor_mul(r_1, r_mu, r_rstd), tok=False)
            t_n2 = E("V", lambda eb: eb.tensor_mul(r_1, r_1, invrr),
                     deps=[t_invrr])
            t_nri = E("V", lambda eb: eb.tensor_scalar_mul(nri, r_1, -1.0))
            stats_guard = [t_var]

            t_b2 = E("T", lambda eb: eb.matmul(bank(2), onef, r_rstd,
                                               start=True, stop=True),
                     deps=[t_rstd, t_var, bc_reads["rstd"], ebank[2]])
            t_b3 = E("T", lambda eb: eb.matmul(bank(3), onef, r_rr,
                                               start=True, stop=True),
                     deps=[t_rr, bc_reads["rr"], ebank[3]])
            t_bcs = E("A", lambda eb: eb.copy(bc_rstd, bank(2)),
                      deps=[t_b2])
            t_bcrr = E("A", lambda eb: eb.copy(bc_rr, bank(3)),
                       deps=[t_b3])
            ebank[2] = t_bcs
            ebank[3] = t_bcrr

            t_xg = []
            for c in range(KC):
                def xgf(eb, c=c):
                    return eb.tensor_scalar_mul(
                        xg[:, csl(c)], xbf[:, csl(c)], g2c[:, c:c + 1])
                t_xg.append(E("V", xgf, deps=[t_mQ, t_xbf]))

            t_alast, pks = wide_phase(
                "aw", lambda k: xg[:, csl(k)], ebank,
                krows=[(rsA, negmu), (c2b, invrr), (oneb, nri)],
                krow_deps=[t_nmu, t_invrr, t_nri],
                rhs_deps=t_xg)
            ev_xa = []
            for m in range(8):
                def axev(eb, m=m):
                    sl = csl(m)
                    eb.tensor_mul(scrv(0), bank(m), bc_rr)
                    eb.tensor_mul(scrv(1), xt[:, sl], bc_rstd)
                    return eb.tensor_add(xabf[:, sl], scrv(0), scrv(1))
                tev = E("V", axev, deps=[t_alast, t_bcs, t_bcrr, t_xtl] +
                        xa_guard)
                ebank[m] = tev
                ev_xa.append(tev)
            bc_reads["rr"] = ev_xa[-1]

            t_qlast, pks = wide_phase(
                "wq", lambda k: xbf[:, csl(k)], ebank,
                krows=[(rsq, negmu)], krow_deps=[t_nmu],
                extra_dep=t_xbf)
            mark_parked("wq", pks)
            ev_q = []
            for m in range(8):
                def qev(eb, m=m):
                    return eb.tensor_mul(qt[:, csl(m)], bank(m), bc_rstd)
                tev = E("V", qev, deps=[t_qlast, t_bcs])
                ebank[m] = tev
                ev_q.append(tev)

            t_vlast, pks = wide_phase(
                "wv", lambda k: xbf[:, csl(k)], ebank,
                krows=[(rsv, negmu)], krow_deps=[t_nmu],
                extra_dep=t_xbf)
            mark_parked("wv", pks)
            ev_v = []
            for m in range(8):
                def vev(eb, m=m):
                    return eb.tensor_mul(vt[:, csl(m)], bank(m), bc_rstd)
                tev = E("V", vev, deps=[t_vlast, t_bcs, t_out] + vt_guard)
                ebank[m] = tev
                ev_v.append(tev)

            t_scan = []
            for c in range(KC):
                def scf(eb, c=c, ti=ti):
                    init = 0.0 if ti == 0 else carry[:, c:c + 1]
                    return eb.tensor_tensor_scan(
                        out=vt[:, csl(c)],
                        data0=lam[:, c:c + 1].broadcast_to([128, T]),
                        data1=vt[:, csl(c)], initial=init,
                        op0=ALU.mult, op1=ALU.add)
                tsc = E("V", scf, deps=[ev_v[c], t_carry0, t_small])
                t_scan.append(tsc)
                def carf(eb, c=c):
                    return eb.tensor_copy(carry[:, c:c + 1],
                                          vt[:, 512 * c + 511:512 * c + 512])
                E("V", carf, deps=[(tsc[0], tsc[1], tsc[2], True)],
                  tok=False)

            t_h = [None] * FKC
            for g in range(4):
                tok, pks = wide_phase(
                    w1nm, lambda k: xabf[:, csl(k)], ebank, base=KC * g,
                    extra_dep=ev_xa[-1])
                for m in range(8):
                    hm = 8 * g + m
                    def hev(eb, m=m, hm=hm, eix=eix):
                        return eb.activation(
                            hbf[:, csl(hm)], bank(m), AF.Gelu,
                            bias=b1c[:, 32 * eix + hm:32 * eix + hm + 1])
                    tev = E("A", hev, deps=[tok] + hbf_guard)
                    ebank[m] = tev
                    t_h[hm] = tev

            t_y = []
            for c in range(KC):
                def yf(eb, c=c):
                    return eb.tensor_mul(vt[:, csl(c)], qt[:, csl(c)],
                                         vt[:, csl(c)])
                t_y.append(E("V", yf, deps=[t_scan[c], ev_q[c]]))
            t_ybf = []
            t_ysq = []
            for c in range(KC):
                def cbf(eb, c=c):
                    return eb.copy(ybf[:, csl(c)], vt[:, csl(c)])
                t_ybf.append(E("A", cbf, deps=[t_y[c]] + ybf_guard))
                def csq(eb, c=c):
                    return eb.activation(xg[:, csl(c)], vt[:, csl(c)],
                                         AF.Square)
                t_ysq.append(E("A", csq, deps=[t_y[c], t_alast]))

            t_mS3 = E("T", mk_stats(2, ybf),
                      deps=t_ybf + [ebank[2]] + stats_guard)
            t_mQ3 = E("T", mk_stats(3, xg),
                      deps=t_ysq + [ebank[3]] + stats_guard)
            xg_guard = [t_mQ3]

            t_mu3 = E("V", lambda eb: eb.tensor_scalar_mul(
                r_2, bankrow(2), 1.0 / D), deps=[t_mS3])
            ebank[2] = t_mu3
            E("V", lambda eb: eb.tensor_mul(r_1, r_2, r_2), tok=False)
            t_var3 = E("V", lambda eb: eb.scalar_tensor_tensor(
                r_1, bankrow(3), 1.0 / D, r_1, op0=ALU.mult,
                op1=ALU.subtract), deps=[t_mQ3])
            t_ln3 = E("A", lambda eb: eb.activation(r_3, r_1, AF.Ln,
                                                    bias=cEPS),
                      deps=[t_var3, t_rstd2, t_invrr])
            t_rstd3 = E("A", lambda eb: eb.activation(
                r_3, r_3, AF.Exp, bias=0.0, scale=cNH))
            stats_guard = [t_var3]
            ebank[3] = t_var3

            t_w2last, pks = wide_phase(
                w2nm, lambda k: hbf[:, csl(k)], ebank, nk=FKC,
                rhs_deps=t_h)
            hbf_guard = [t_w2last]

            ev_eo = []
            for m in range(8):
                def eoev(eb, m=m, eix=eix):
                    return eb.activation(
                        eobf[:, csl(m)], bank(m), AF.Identity,
                        bias=b2c[:, 8 * eix + m:8 * eix + m + 1])
                tev = E("A", eoev, deps=[t_w2last] + eo_guard)
                ebank[m] = tev
                ev_eo.append(tev)

            t_bm3m = E("T", lambda eb: eb.matmul(bank(2), onef, r_2,
                                                 start=True, stop=True),
                       deps=[t_mu3, t_var3, bc_reads["m3"], ebank[2]])
            t_br3m = E("T", lambda eb: eb.matmul(bank(3), onef, r_3,
                                                 start=True, stop=True),
                       deps=[t_rstd3, bc_reads["r3"], ebank[3]])
            t_bm3 = E("A", lambda eb: eb.copy(bc_m3, bank(2)),
                      deps=[t_bm3m])
            t_br3 = E("A", lambda eb: eb.copy(bc_r3, bank(3)),
                      deps=[t_br3m])
            ebank[2] = t_bm3
            ebank[3] = t_br3

            t_yln = []
            for c in range(KC):
                def l3a(eb, c=c):
                    sl = csl(c)
                    eb.tensor_sub(qt[:, sl], vt[:, sl], bc_m3)
                    eb.tensor_mul(qt[:, sl], qt[:, sl], bc_r3)
                    return eb.tensor_scalar(qt[:, sl], qt[:, sl],
                                            g3c[:, c:c + 1],
                                            b3c[:, c:c + 1], op0=ALU.mult,
                                            op1=ALU.add)
                t_yln.append(E("V", l3a, deps=[t_bm3, t_br3, t_mS3]))
            bc_reads["m3"] = t_yln[-1]
            bc_reads["r3"] = t_yln[-1]

            t_glast, pks = wide_phase(
                "wg", lambda k: xbf[:, csl(k)], ebank,
                krows=[(rsg, negmu)], krow_deps=[t_nmu],
                extra_dep=t_xbf)
            mark_parked("wg", pks)
            t_yg = []
            for m in range(8):
                def gmul(eb, m=m):
                    return eb.tensor_mul(scrv(m % 2), bank(m), bc_rstd)
                t_gm = E("V", gmul, deps=[t_glast, t_bcs, prev_scra[m % 2]])
                def gtnh(eb, m=m):
                    return eb.activation(scra(m % 2), scrv(m % 2), AF.Tanh,
                                         bias=0.0, scale=cPH128)
                t_gs = E("A", gtnh, deps=[t_gm, prev_scra[m % 2]])
                def gu(eb, m=m):
                    return eb.scalar_tensor_tensor(
                        scrv(m % 2), scra(m % 2), 1.0, scrv(m % 2),
                        op0=ALU.add, op1=ALU.mult)
                t_gu = E("V", gu, deps=[t_gs])
                def ygf(eb, m=m):
                    return eb.tensor_mul(ybf[:, csl(m)], qt[:, csl(m)],
                                         scrv(m % 2))
                t_ygm = E("V", ygf, deps=[t_gu, t_yln[m], t_mS3])
                ebank[m] = t_ygm
                prev_scra[m % 2] = t_ygm
                t_yg.append(t_ygm)
            bc_reads["rstd"] = t_yg[-1]
            xbf_guard = [t_glast, t_xg[-1]]

            t_olast, pks = wide_phase(
                "wo", lambda k: ybf[:, csl(k)], ebank,
                rhs_deps=t_yg)
            mark_parked("wo", pks)
            ev_s = []
            for m in range(8):
                def oev(eb, m=m):
                    return eb.activation(stt[:, csl(m)], bank(m),
                                         AF.Identity, bias=0.0,
                                         scale=cQT128)
                tev = E("A", oev, deps=[t_olast] + stt_guard)
                ebank[m] = tev
                ev_s.append(tev)
            ybf_guard = [t_olast]

            t_nxlast, pks = wide_phase(
                "nw", lambda k: eobf[:, csl(k)], ebank,
                rhs_deps=ev_eo)
            mark_parked("nw", pks)
            eo_guard = [t_nxlast]

            t_fin = []
            for m in range(8):
                fs = 2 * (m % 2)
                def fgel(eb, m=m, fs=fs):
                    return eb.activation(scra(fs), bank(m), AF.Gelu,
                                         bias=nbc[:, m:m + 1])
                E("A", fgel, deps=[t_nxlast, prev_scra[m % 2]], tok=False)
                def ftnh(eb, m=m, fs=fs):
                    return eb.activation(scra(fs + 1), scra(fs), AF.Tanh,
                                         bias=0.0, scale=cPH128)
                t_gg = E("A", ftnh)
                ebank[m] = t_gg
                def ffin(eb, m=m, fs=fs):
                    sl = csl(m)
                    eb.scalar_tensor_tensor(
                        scrv(m % 2), scra(fs + 1), 1.0, scra(fs),
                        op0=ALU.add, op1=ALU.mult)
                    eb.tensor_mul(scrv(m % 2), scrv(m % 2), stt[:, sl])
                    return eb.tensor_add(vt[:, sl], scrv(m % 2), xt[:, sl])
                t_fv = E("V", ffin,
                         deps=[t_gg, t_xtl, t_ybf[-1], t_ysq[-1]])
                t_fin.append(t_fv)
                prev_scra[m % 2] = t_fv
            stt_guard = [t_fin[-1]]
            xt_guard = [t_fin[-1], ev_xa[-1]]
            xa_guard = [t_h[-1]]

            def outdma(eb, t0=t0):
                return eb.dma_start(
                    out=outT[:, t0:t0 + T].rearrange("(c p) t -> p c t",
                                                     p=128),
                    in_=vt.rearrange("p (c t) -> p c t", c=KC))
            t_out = E("S", outdma, deps=t_fin, d=16, sem="OU")
            vt_guard = [t_out]
            if ti + 1 < NT:
                pre_x[ti + 1] = (
                    E("G", xload_bf(T * (ti + 1)), deps=xbf_guard, d=16,
                      sem="GX"),
                    E("S", xload_f32(T * (ti + 1)), deps=xt_guard, d=16,
                      sem="XL"))
            row_guard = [t_glast, t_alast, t_b2, t_b3, t_bm3m, t_br3m]

        E("S", lambda eb: eb.engine_nop() if hasattr(eb, "engine_nop")
          else eb.nop(), deps=[t_out], tok=False)

        def emit_for(name):
            def body(eb):
                for (eng, fn, waits, incs) in prog.ops:
                    if eng != name:
                        continue
                    for sname in sorted(waits):
                        eb.wait_ge(sems[sname], waits[sname])
                    ins = fn(eb)
                    for sname, dv in incs:
                        ins.then_inc(sems[sname], dv)
            return body

        block.tensor(emit_for("T"))
        block.vector(emit_for("V"))
        block.scalar(emit_for("A"))
        block.sync(emit_for("S"))
        block.gpsimd(emit_for("G"))

    return nc


_CACHE = {}


def _get_nc():
    if "nc" not in _CACHE:
        _CACHE["nc"] = build_nc()
    return _CACHE["nc"]


def _vec_cols(v):
    return np.ascontiguousarray(v.reshape(8, 128).T)


def kernel(
    L0_attn_x, L0_attn_decay, L0_attn_Wq, L0_attn_Wv, L0_attn_Wo, L0_attn_Wg,
    L0_attn_ln_gamma, L0_attn_ln_beta, L0_ffn_ln_scale, L0_ffn_ln_bias,
    L0_ffn_attn_w, L0_ffn_attn_b, L0_ffn_gate_w, L0_ffn_w1, L0_ffn_b1,
    L0_ffn_w2, L0_ffn_b2, L0_ffn_next_w, L0_ffn_next_b,
):
    f = np.float32
    x = np.asarray(L0_attn_x, f)
    decay = np.asarray(L0_attn_decay, f)
    shared = {
        "wqT": np.ascontiguousarray(np.asarray(L0_attn_Wq, f).T),
        "wvT": np.ascontiguousarray(np.asarray(L0_attn_Wv, f).T),
        "wgT": np.ascontiguousarray(np.asarray(L0_attn_Wg, f).T),
        "woT": np.ascontiguousarray(np.asarray(L0_attn_Wo, f).T),
        "awT": np.ascontiguousarray(np.asarray(L0_ffn_attn_w, f).T),
        "nwT": np.ascontiguousarray(np.asarray(L0_ffn_next_w, f).T),
        "abrow": np.ascontiguousarray(np.asarray(L0_ffn_attn_b, f)[None, :]),
        "nbc": _vec_cols(np.asarray(L0_ffn_next_b, f)),
        "g3c": _vec_cols(np.asarray(L0_attn_ln_gamma, f)),
        "b3c": _vec_cols(np.asarray(L0_attn_ln_beta, f)),
        "g2c": _vec_cols(np.asarray(L0_ffn_ln_scale, f)),
        "b2vc": _vec_cols(np.asarray(L0_ffn_ln_bias, f)),
        "lamh": np.ascontiguousarray(
            decay[(2 * np.arange(8)[None, :] +
                   (np.arange(128) // 64)[:, None])]),
    }
    w1 = np.asarray(L0_ffn_w1, f)
    w2 = np.asarray(L0_ffn_w2, f)
    b1 = np.asarray(L0_ffn_b1, f)
    b2 = np.asarray(L0_ffn_b2, f)
    in_maps = []
    for c in range(NCORES):
        e0 = (c % 4) * 2
        m = dict(shared)
        m["xT"] = np.ascontiguousarray(x[c].T)
        m["w1p"] = np.ascontiguousarray(w1[e0:e0 + 2])
        m["w2p"] = np.ascontiguousarray(w2[e0:e0 + 2])
        m["b1c"] = np.ascontiguousarray(
            np.concatenate([b1[e0 + e].reshape(32, 128).T for e in range(2)],
                           axis=1))
        m["b2c"] = np.ascontiguousarray(
            np.concatenate([b2[e0 + e].reshape(8, 128).T for e in range(2)],
                           axis=1))
        in_maps.append(m)

    nc = _get_nc()
    res = run_bass_kernel_spmd(nc, in_maps, core_ids=list(range(NCORES)))
    out = np.stack([np.asarray(res.results[c]["outT"]).T
                    for c in range(NCORES)])
    return np.ascontiguousarray(out.astype(np.float32))


# version 11
# speedup vs baseline: 1.0123x; 1.0123x over previous
import numpy as np
import concourse.bass as bass
import concourse.mybir as mybir
from concourse.bass_utils import run_bass_kernel_spmd

F32 = mybir.dt.float32
BF16 = mybir.dt.bfloat16
U8 = mybir.dt.uint8
AF = mybir.ActivationFunctionType
ALU = mybir.AluOpType

D = 1024
S = 4096
T = 512
NT = S // T
KC = D // 128
FH = 4 * D
FKC = FH // 128
EPS = 1e-5
NCORES = 8

RF_RSTD = 0
RF_RR = 512
RF_MU = 1024
RF_R1 = 1536
RF_R2 = 2048
RF_R3 = 2560
RF_ONEF = 3072
ROWSF_ELEMS = 3200
RF_ABROW = RF_MU
RF_C2F = RF_R2
RB_NEGMU = 0
RB_INVRR = 512
RB_NRI = 1024
RB_RSQ = 1536
RB_RSV = 2560
RB_RSG = 3584
RB_RSA = 4608
RB_C2 = 5632
RB_ONEB = 6656
ROWSB_ELEMS = 6784
NRING = 10
PF = 7


class P:

    SEMS = (["T", "V", "A", "SM", "GX", "XL", "OU"] +
            ["GIN%d" % i for i in range(10)] +
            ["RGS%d" % i for i in range(10)] +
            ["PKS%d" % i for i in range(10)])

    def __init__(self):
        self.ops = []
        self.cnt = {n: 0 for n in self.SEMS}

    def emit(self, eng, fn, deps=(), tok=True, d=1, sem=None):
        sem = sem or eng
        waits = {}
        for t in deps:
            if t is None:
                continue
            te, ts, tv, ta = t
            if te == eng and not ta:
                continue
            if waits.get(ts, 0) < tv:
                waits[ts] = tv
        token = None
        incs = []
        if tok:
            self.cnt[sem] += d
            incs = [(sem, d)]
            token = (eng, sem, self.cnt[sem], d == 16)
        self.ops.append((eng, fn, waits, incs))
        return token


def build_nc():
    nc = bass.Bass("TRN2", target_bir_lowering=False,
                   detect_race_conditions=False)

    xT = nc.dram_tensor("xT", [D, S], F32, kind="ExternalInput").ap()
    wqT = nc.dram_tensor("wqT", [D, D], F32, kind="ExternalInput").ap()
    wvT = nc.dram_tensor("wvT", [D, D], F32, kind="ExternalInput").ap()
    wgT = nc.dram_tensor("wgT", [D, D], F32, kind="ExternalInput").ap()
    woT = nc.dram_tensor("woT", [D, D], F32, kind="ExternalInput").ap()
    awT = nc.dram_tensor("awT", [D, D], F32, kind="ExternalInput").ap()
    nwT = nc.dram_tensor("nwT", [D, D], F32, kind="ExternalInput").ap()
    w1p = nc.dram_tensor("w1p", [2, D, FH], F32, kind="ExternalInput").ap()
    w2p = nc.dram_tensor("w2p", [2, FH, D], F32, kind="ExternalInput").ap()
    b1c_d = nc.dram_tensor("b1c", [128, 64], F32, kind="ExternalInput").ap()
    b2c_d = nc.dram_tensor("b2c", [128, 16], F32, kind="ExternalInput").ap()
    abrow_d = nc.dram_tensor("abrow", [1, D], F32, kind="ExternalInput").ap()
    nbc_d = nc.dram_tensor("nbc", [128, 8], F32, kind="ExternalInput").ap()
    g3c_d = nc.dram_tensor("g3c", [128, 8], F32, kind="ExternalInput").ap()
    b3c_d = nc.dram_tensor("b3c", [128, 8], F32, kind="ExternalInput").ap()
    g2c_d = nc.dram_tensor("g2c", [128, 8], F32, kind="ExternalInput").ap()
    b2vc_d = nc.dram_tensor("b2vc", [128, 8], F32, kind="ExternalInput").ap()
    lam_d = nc.dram_tensor("lamh", [128, 8], F32, kind="ExternalInput").ap()
    outT = nc.dram_tensor("outT", [D, S], F32, kind="ExternalOutput").ap()
    pw = {}
    for nm in ["aw", "wq", "wv", "wg", "wo", "nw"]:
        pw[nm] = nc.dram_tensor(nm + "b", [D, D], BF16, kind="Internal").ap()
    w1b_d = nc.dram_tensor("w1b", [2, FKC, 128, 1024], BF16,
                           kind="Internal").ap()
    w2b_d = nc.dram_tensor("w2b", [2, FH, D], BF16, kind="Internal").ap()
    consts_np = np.zeros((128, 8), dtype=np.float32)
    consts_np[:, 0] = EPS
    consts_np[:, 1] = -0.5
    consts_np[:, 2] = 0.5
    consts_np[:, 3] = 0.25
    consts_d = nc.inline_tensor(consts_np, name="constsrom").ap()

    prog = P()
    E = prog.emit

    from contextlib import ExitStack
    with ExitStack() as stack:
        en = stack.enter_context
        rows_f = en(nc.sbuf_tensor("rowsf", [1, ROWSF_ELEMS], F32))
        rows_b = en(nc.sbuf_tensor("rowsb", [1, ROWSB_ELEMS], BF16))
        scrv_t = en(nc.sbuf_tensor("scrv", [128, 1024], F32))
        scra_t = en(nc.sbuf_tensor("scra", [128, 2048], F32))
        b1c = en(nc.sbuf_tensor("b1cs", [128, 64], F32))
        b2c = en(nc.sbuf_tensor("b2cs", [128, 16], F32))
        nbc = en(nc.sbuf_tensor("nbcs", [128, 8], F32))
        g3c = en(nc.sbuf_tensor("g3cs", [128, 8], F32))
        b3c = en(nc.sbuf_tensor("b3cs", [128, 8], F32))
        g2c = en(nc.sbuf_tensor("g2cs", [128, 8], F32))
        b2vc = en(nc.sbuf_tensor("b2vcs", [128, 8], F32))
        g2cb = en(nc.sbuf_tensor("g2cbs", [128, 8], BF16))
        b2vcb = en(nc.sbuf_tensor("b2vcbs", [128, 8], BF16))
        lam = en(nc.sbuf_tensor("lams", [128, 8], F32))
        carry = en(nc.sbuf_tensor("carrys", [128, 8], F32))
        cons = en(nc.sbuf_tensor("constss", [128, 8], F32))
        onescol = en(nc.sbuf_tensor("onescol", [128, 1], BF16))
        hbf_t = en(nc.sbuf_tensor("hbft", [128, 16384], BF16))
        xt_t = en(nc.sbuf_tensor("xtt", [128, 4096], F32))
        stt_t = en(nc.sbuf_tensor("stts", [128, 4096], F32))
        qt_t = en(nc.sbuf_tensor("qtt", [128, 4096], F32))
        vt_t = en(nc.sbuf_tensor("vtt", [128, 4096], F32))
        xbf_t = en(nc.sbuf_tensor("xbft", [128, 4096], BF16))
        ybf_t = en(nc.sbuf_tensor("ybft", [128, 4096], BF16))
        xg_t = en(nc.sbuf_tensor("xgt", [128, 4096], BF16))
        xabf_t = en(nc.sbuf_tensor("xabft", [128, 4096], BF16))
        eobf_t = en(nc.sbuf_tensor("eobft", [128, 4096], BF16))
        ring_t = en(nc.sbuf_tensor("ringt", [128, NRING * 1024], BF16))
        bc_t = en(nc.sbuf_tensor("bct", [128, 2048], F32))
        PS = en(nc.psum_tensor("PS", [128, 4096], F32))
        sems = {nm: en(nc.semaphore("sem" + nm)) for nm in P.SEMS}
        block = en(nc.Block())

        def row(off, n, dt=F32):
            t = rows_f if dt == F32 else rows_b
            return t[:, off:off + n]

        def scrv(i):
            return scrv_t[:, i * 512:(i + 1) * 512]

        def scra(i):
            return scra_t[:, i * 512:(i + 1) * 512]

        def bank(b):
            return PS[:, 512 * b:512 * (b + 1)]

        def bankrow(b):
            return PS[0:1, 512 * b:512 * (b + 1)]

        cEPS = cons[0:1, 0:1]
        cNH = cons[0:1, 1:2]
        cPH = cons[0:1, 2:3]
        cQT = cons[0:1, 3:4]
        cPH128 = cons[:, 2:3]
        cQT128 = cons[:, 3:4]

        hbf = hbf_t[:, :]
        xt = xt_t[:, :]
        stt = stt_t[:, :]
        qt = qt_t[:, :]
        vt = vt_t[:, :]
        xbf = xbf_t[:, :]
        ybf = ybf_t[:, :]
        xg = xg_t[:, :]
        xabf = xabf_t[:, :]
        eobf = eobf_t[:, :]
        ring = [ring_t[:, 1024 * i:1024 * (i + 1)] for i in range(NRING)]
        bc_rstd = bc_t[:, 0:512]
        bc_rr = bc_t[:, 512:1024]
        bc_m3 = bc_t[:, 1024:1536]
        bc_r3 = bc_t[:, 1536:2048]

        negmu = row(RB_NEGMU, T, BF16)
        invrr = row(RB_INVRR, T, BF16)
        nri = row(RB_NRI, T, BF16)
        r_rstd = row(RF_RSTD, T)
        r_rr = row(RF_RR, T)
        r_mu = row(RF_MU, T)
        r_1 = row(RF_R1, T)
        r_2 = row(RF_R2, T)
        r_3 = row(RF_R3, T)
        rsq = row(RB_RSQ, 1024, BF16)
        rsv = row(RB_RSV, 1024, BF16)
        rsg = row(RB_RSG, 1024, BF16)
        rsA = row(RB_RSA, 1024, BF16)
        c2b = row(RB_C2, 1024, BF16)
        onef = row(RF_ONEF, 128)
        oneb = row(RB_ONEB, 128, BF16)
        abrow = row(RF_ABROW, 1024)
        c2f = row(RF_C2F, 1024)

        def dm(dst, src):
            return lambda eb: eb.dma_start(out=dst, in_=src)

        def csl(c):
            return slice(512 * c, 512 * (c + 1))

        t_small = None
        pre_x = {}
        for dst, src in [(b1c[:], b1c_d), (b2c[:], b2c_d), (nbc[:], nbc_d),
                         (g3c[:], g3c_d), (b3c[:], b3c_d), (g2c[:], g2c_d),
                         (b2vc[:], b2vc_d), (lam[:], lam_d),
                         (abrow, abrow_d), (cons[:], consts_d)]:
            t_small = E("S", dm(dst, src), d=16, sem="SM")

        E("V", lambda eb: eb.memset(onescol[:], 1.0))
        E("V", lambda eb: eb.memset(onef, 1.0))
        t_ones = E("V", lambda eb: eb.memset(oneb, 1.0))
        t_carry0 = E("V", lambda eb: eb.memset(carry[:], 0.0))
        t_g2b = E("V", lambda eb: eb.tensor_copy(g2cb[:], g2c[:]),
                  deps=[t_small])
        t_b2b = E("V", lambda eb: eb.tensor_copy(b2vcb[:], b2vc[:]),
                  deps=[t_small])

        def xload_f32(t0):
            return lambda eb: eb.dma_start(
                out=xt.rearrange("p (c t) -> p c t", c=KC),
                in_=xT[:, t0:t0 + T].rearrange("(c p) t -> p c t", p=128))

        def xload_bf(t0):
            return lambda eb: eb.dma_start(
                out=xbf.rearrange("p (c t) -> p c t", c=KC),
                in_=xT[:, t0:t0 + T].rearrange("(c p) t -> p c t", p=128))

        pre_x[0] = (E("G", xload_bf(0), d=16, sem="GX"),
                    E("S", xload_f32(0), d=16, sem="XL"))

        def mk_nonmoe(fp, pk):
            return (lambda k: fp[128 * k:128 * (k + 1), :],
                    lambda k: pk[128 * k:128 * (k + 1), :])

        def mk_w1(e):
            def f32s(j):
                k, g = j % KC, j // KC
                return w1p[e, 128 * k:128 * (k + 1),
                           1024 * g:1024 * (g + 1)]
            def pks(j):
                return w1b_d[e, j]
            return (f32s, pks)

        def mk_w2(e):
            return (lambda k: w2p[e, 128 * k:128 * (k + 1), :],
                    lambda k: w2b_d[e, 128 * k:128 * (k + 1), :])

        SRC = {
            "aw": mk_nonmoe(awT, pw["aw"]),
            "wq": mk_nonmoe(wqT, pw["wq"]),
            "wv": mk_nonmoe(wvT, pw["wv"]),
            "wg": mk_nonmoe(wgT, pw["wg"]),
            "wo": mk_nonmoe(woT, pw["wo"]),
            "nw": mk_nonmoe(nwT, pw["nw"]),
            "w10": mk_w1(0), "w11": mk_w1(1),
            "w20": mk_w2(0), "w21": mk_w2(1),
        }
        parked = set()
        park_last = {}
        ring_state = {"idx": 0, "free": [None] * NRING}

        def ringload(name, k):
            ri = ring_state["idx"] % NRING
            ring_state["idx"] += 1
            fr = ring_state["free"][ri]
            fr = fr if isinstance(fr, list) else [fr]
            assert name in parked, name
            t_ld = E("S", dm(ring[ri], pkf_of[name](k)),
                     deps=fr + [park_last.get(name)], d=16,
                     sem="RGS%d" % ri)
            return ri, t_ld, None

        pkf_of = {nm: SRC[nm][1] for nm in SRC}

        def park_stream(name, nchunks):
            f32f, pkf = SRC[name]
            pks = []
            for j in range(nchunks):
                ri = ring_state["idx"] % NRING
                ring_state["idx"] += 1
                fr = ring_state["free"][ri]
                fr = fr if isinstance(fr, list) else [fr]
                t_ld = E("G", dm(ring[ri], f32f(j)), deps=fr, d=16,
                         sem="GIN%d" % ri)
                t_pk = E("S", dm(pkf(j), ring[ri]), deps=[t_ld], d=16,
                         sem="PKS%d" % ri)
                ring_state["free"][ri] = [t_pk]
                pks.append(t_pk)
            parked.add(name)
            park_last[name] = pks[-1]

        def park_stream2(name, base, n):
            f32f, pkf = SRC[name]
            pks = []
            for j in range(base, base + n):
                ri = ring_state["idx"] % NRING
                ring_state["idx"] += 1
                fr = ring_state["free"][ri]
                fr = fr if isinstance(fr, list) else [fr]
                t_ld = E("G", dm(ring[ri], f32f(j)), deps=fr, d=16,
                         sem="GIN%d" % ri)
                t_pk = E("S", dm(pkf(j), ring[ri]), deps=[t_ld], d=16,
                         sem="PKS%d" % ri)
                ring_state["free"][ri] = [t_pk]
                pks.append(t_pk)
            park_last[name + "_p%d" % base] = pks[-1]
            if base + n == FKC:
                parked.add(name)
                park_last[name] = pks[-1]

        def mark_parked(name, pks):
            if pks and name not in parked:
                parked.add(name)
                park_last[name] = pks[-1]

        def wide_phase(name, rhs_fn, ebank, base=0, nk=KC, krows=None,
                       krow_deps=(), rhs_deps=None, extra_dep=None):
            pks = []
            lds = []
            for kk in range(min(PF, nk)):
                lds.append(ringload(name, base + kk))
            tok = None
            for k in range(nk):
                if k + PF < nk:
                    lds.append(ringload(name, base + k + PF))
                ri, t_ld, t_pk = lds[k]
                if t_pk is not None:
                    pks.append(t_pk)
                rdep = rhs_deps[k] if rhs_deps else None
                pkfree = [t_pk] if t_pk is not None else []
                if k == 0:
                    for m in range(8):
                        def mm0(eb, ri=ri, m=m):
                            return eb.matmul(
                                bank(m), ring[ri][:, 128 * m:128 * (m + 1)],
                                rhs_fn(0), start=True, stop=False)
                        tok = E("T", mm0, deps=[t_ld, ebank[m], rdep,
                                                extra_dep])
                else:
                    def mmk(eb, ri=ri, k=k):
                        ins = None
                        for m in range(8):
                            ins = eb.matmul(
                                bank(m), ring[ri][:, 128 * m:128 * (m + 1)],
                                rhs_fn(k), start=False,
                                stop=(krows is None and k == nk - 1))
                        return ins
                    tok = E("T", mmk, deps=[t_ld, rdep, extra_dep])
                ring_state["free"][ri] = [tok] + pkfree
            if krows is not None:
                def krf(eb):
                    ins = None
                    for m in range(8):
                        for i, (lh, rh) in enumerate(krows):
                            lhs = (lh[0:1, 128 * m:128 * (m + 1)]
                                   if lh.shape[1] >= 1024 else lh)
                            ins = eb.matmul(
                                bank(m), lhs, rh,
                                start=False, stop=(i == len(krows) - 1))
                    return ins
                tok = E("T", krf, deps=list(krow_deps))
            return tok, pks

        def prologue_rows(name, emit_mm, prev_bank_read):
            pks = []
            toks = []
            f32f, _pkf = SRC[name]
            for half in range(2):
                lds = []
                for kk in range(4):
                    k = 4 * half + kk
                    ri = ring_state["idx"] % NRING
                    ring_state["idx"] += 1
                    fr = ring_state["free"][ri]
                    fr = fr if isinstance(fr, list) else [fr]
                    t_ld = E("G", dm(ring[ri], f32f(k)), deps=fr, d=16,
                             sem="GIN%d" % ri)
                    t_pk = E("S", dm(_pkf(k), ring[ri]), deps=[t_ld],
                             d=16, sem="PKS%d" % ri)
                    lds.append((ri, t_ld, t_pk))
                def mmf(eb, lds=lds, half=half):
                    ins = None
                    for kk in range(4):
                        k = 4 * half + kk
                        ins = emit_mm(eb, k, lds[kk][0], k == 0, k == KC - 1)
                    return ins
                tk = E("T", mmf, deps=[t[1] for t in lds] +
                       [t_ones, prev_bank_read, t_g2b, t_b2b])
                for t in lds:
                    ring_state["free"][t[0]] = ([tk, t[2]]
                                                if t[2] is not None else tk)
                    if t[2] is not None:
                        pks.append(t[2])
                toks.append(tk)
            return toks[-1], pks

        def mk_aw(eb, k, ri, first, last):
            ins = None
            for h in range(2):
                eb.matmul(bankrow(6 + h), g2cb[:, k:k + 1],
                          ring[ri][:, 512 * h:512 * (h + 1)],
                          start=first, stop=last)
                ins = eb.matmul(bankrow(4 + h), b2vcb[:, k:k + 1],
                                ring[ri][:, 512 * h:512 * (h + 1)],
                                start=first, stop=last)
            return ins

        def mk_rs(eb, k, ri, first, last):
            ins = None
            for h in range(2):
                ins = eb.matmul(bankrow(6 + h), onescol[:],
                                ring[ri][:, 512 * h:512 * (h + 1)],
                                start=first, stop=last)
            return ins

        bank_read = t_ones
        rs_out = {}
        t_awmm, aw_pks = prologue_rows("aw", mk_aw, bank_read)
        def ev_rsa(eb):
            eb.copy(rsA[0:1, 0:512], bankrow(6))
            return eb.copy(rsA[0:1, 512:1024], bankrow(7))
        t_rsa = E("A", ev_rsa, deps=[t_awmm])
        def ev_c2(eb):
            eb.tensor_copy(c2f[0:1, 0:512], bankrow(4))
            eb.tensor_copy(c2f[0:1, 512:1024], bankrow(5))
            eb.tensor_add(c2f, c2f, abrow)
            return eb.tensor_copy(c2b, c2f)
        t_c2 = E("V", ev_c2, deps=[t_awmm, t_small])
        mark_parked("aw", aw_pks)
        bank_read = t_rsa
        for nm, dstrow in [("wq", rsq), ("wv", rsv)]:
            tk, pks = prologue_rows(nm, mk_rs, bank_read)
            def evf(eb, dstrow=dstrow):
                eb.copy(dstrow[0:1, 0:512], bankrow(6))
                return eb.copy(dstrow[0:1, 512:1024], bankrow(7))
            bank_read = E("A", evf, deps=[tk])
            rs_out[nm] = bank_read
            mark_parked(nm, pks)

        park_stream("w10", FKC)
        park_stream("w20", FKC)
        for nm, dstrow in [("wg", rsg)]:
            tk, pks = prologue_rows(nm, mk_rs, bank_read)
            def evf(eb, dstrow=dstrow):
                eb.copy(dstrow[0:1, 0:512], bankrow(6))
                return eb.copy(dstrow[0:1, 512:1024], bankrow(7))
            bank_read = E("A", evf, deps=[tk])
            rs_out[nm] = bank_read
            mark_parked(nm, pks)
        park_stream("wo", KC)
        park_stream("nw", KC)

        ebank = {0: t_ones, 1: t_ones, 2: t_ones, 3: t_ones,
                 4: t_c2, 5: t_c2, 6: t_rsa, 7: t_rsa}
        xbf_guard = [None]
        xg_guard = [None]
        ybf_guard = [None]
        xa_guard = [None]
        hbf_guard = [None]
        eo_guard = [None]
        stt_guard = [None]
        xt_guard = [None]
        vt_guard = [None]
        row_guard = [t_c2]
        stats_guard = [t_c2]
        bc_reads = {"rstd": t_rsa, "rr": t_rsa, "m3": t_rsa, "r3": t_rsa}
        prev_scra = [None, None]
        t_out = None
        w1_pks = {}

        for ti in range(NT):
            t0 = ti * T
            eix = ti // 4
            w1nm = "w1" + str(eix)
            w2nm = "w2" + str(eix)

            t_xbf, t_xtl = pre_x[ti]
            if ti < 4:
                park_stream2("w11", 8 * ti, 8)
                park_stream2("w21", 8 * ti, 8)

            def sqf(eb):
                ins = None
                for c in range(KC):
                    ins = eb.activation(xg[:, csl(c)], xbf[:, csl(c)],
                                        AF.Square)
                return ins
            t_sq = E("A", sqf, deps=[t_xbf] + xg_guard)

            def mk_stats(bk, src):
                def mmf(eb, bk=bk, src=src):
                    ins = None
                    for k in range(KC):
                        ins = eb.matmul(bankrow(bk), onescol[:],
                                        src[:, csl(k)],
                                        start=(k == 0), stop=(k == KC - 1))
                    return ins
                return mmf
            t_mS = E("T", mk_stats(2, xbf),
                     deps=[t_xbf, ebank[2], t_ones] + stats_guard)
            t_mQ = E("T", mk_stats(3, xg),
                     deps=[t_sq, ebank[3]] + stats_guard)

            t_mu = E("V", lambda eb: eb.tensor_scalar_mul(
                r_mu, bankrow(2), 1.0 / D), deps=[t_mS] + row_guard)
            t_nmu = E("V", lambda eb: eb.tensor_scalar_mul(negmu, r_mu,
                                                           -1.0))
            E("V", lambda eb: eb.tensor_mul(r_1, r_mu, r_mu), tok=False)
            t_var = E("V", lambda eb: eb.scalar_tensor_tensor(
                r_2, bankrow(3), 1.0 / D, r_1, op0=ALU.mult,
                op1=ALU.subtract), deps=[t_mQ])
            t_ln1 = E("A", lambda eb: eb.activation(r_3, r_2, AF.Ln,
                                                    bias=cEPS),
                      deps=[t_var] + row_guard)
            t_rstd = E("A", lambda eb: eb.activation(
                r_rstd, r_3, AF.Exp, bias=0.0, scale=cNH))
            E("V", lambda eb: eb.tensor_mul(r_1, r_rstd, r_rstd),
              deps=[t_rstd], tok=False)
            t_var2 = E("V", lambda eb: eb.tensor_mul(r_1, r_1, r_2))
            t_ln2 = E("A", lambda eb: eb.activation(r_2, r_1, AF.Ln,
                                                    bias=cEPS),
                      deps=[t_var2])
            t_lnsum = E("V", lambda eb: eb.tensor_add(r_1, r_3, r_2),
                        deps=[t_ln2])
            t_invrr = E("A", lambda eb: eb.activation(
                invrr, r_1, AF.Exp, bias=0.0, scale=cPH), deps=[t_lnsum])
            t_rstd2 = E("A", lambda eb: eb.activation(
                r_2, r_2, AF.Exp, bias=0.0, scale=cNH), deps=[t_lnsum])
            t_rr = E("V", lambda eb: eb.tensor_mul(r_rr, r_rstd, r_2),
                     deps=[t_rstd2])
            E("V", lambda eb: eb.tensor_mul(r_1, r_mu, r_rstd), tok=False)
            t_n2 = E("V", lambda eb: eb.tensor_mul(r_1, r_1, invrr),
                     deps=[t_invrr])
            t_nri = E("V", lambda eb: eb.tensor_scalar_mul(nri, r_1, -1.0))
            stats_guard = [t_var]

            t_b2 = E("T", lambda eb: eb.matmul(bank(2), onef, r_rstd,
                                               start=True, stop=True),
                     deps=[t_rstd, t_var, bc_reads["rstd"], ebank[2]])
            t_b3 = E("T", lambda eb: eb.matmul(bank(3), onef, r_rr,
                                               start=True, stop=True),
                     deps=[t_rr, bc_reads["rr"], ebank[3]])
            t_bcs = E("A", lambda eb: eb.copy(bc_rstd, bank(2)),
                      deps=[t_b2])
            t_bcrr = E("A", lambda eb: eb.copy(bc_rr, bank(3)),
                       deps=[t_b3])
            ebank[2] = t_bcs
            ebank[3] = t_bcrr

            t_xg = []
            for c in range(KC):
                def xgf(eb, c=c):
                    return eb.tensor_scalar_mul(
                        xg[:, csl(c)], xbf[:, csl(c)], g2c[:, c:c + 1])
                t_xg.append(E("V", xgf, deps=[t_mQ, t_xbf]))

            t_alast, pks = wide_phase(
                "aw", lambda k: xg[:, csl(k)], ebank,
                krows=[(rsA, negmu), (c2b, invrr), (oneb, nri)],
                krow_deps=[t_nmu, t_invrr, t_nri],
                rhs_deps=t_xg)
            ev_xa = []
            for m in range(8):
                def axev(eb, m=m):
                    sl = csl(m)
                    eb.tensor_mul(scrv(0), bank(m), bc_rr)
                    eb.tensor_mul(scrv(1), xt[:, sl], bc_rstd)
                    return eb.tensor_add(xabf[:, sl], scrv(0), scrv(1))
                tev = E("V", axev, deps=[t_alast, t_bcs, t_bcrr, t_xtl] +
                        xa_guard)
                ebank[m] = tev
                ev_xa.append(tev)
            bc_reads["rr"] = ev_xa[-1]

            t_qlast, pks = wide_phase(
                "wq", lambda k: xbf[:, csl(k)], ebank,
                krows=[(rsq, negmu)], krow_deps=[t_nmu],
                extra_dep=t_xbf)
            mark_parked("wq", pks)
            ev_q = []
            for m in range(8):
                def qev(eb, m=m):
                    return eb.tensor_mul(qt[:, csl(m)], bank(m), bc_rstd)
                tev = E("V", qev, deps=[t_qlast, t_bcs])
                ebank[m] = tev
                ev_q.append(tev)

            t_vlast, pks = wide_phase(
                "wv", lambda k: xbf[:, csl(k)], ebank,
                krows=[(rsv, negmu)], krow_deps=[t_nmu],
                extra_dep=t_xbf)
            mark_parked("wv", pks)
            ev_v = []
            for m in range(8):
                def vev(eb, m=m):
                    return eb.tensor_mul(vt[:, csl(m)], bank(m), bc_rstd)
                tev = E("V", vev, deps=[t_vlast, t_bcs, t_out] + vt_guard)
                ebank[m] = tev
                ev_v.append(tev)

            t_scan = []
            for c in range(KC):
                def scf(eb, c=c, ti=ti):
                    init = 0.0 if ti == 0 else carry[:, c:c + 1]
                    return eb.tensor_tensor_scan(
                        out=vt[:, csl(c)],
                        data0=lam[:, c:c + 1].broadcast_to([128, T]),
                        data1=vt[:, csl(c)], initial=init,
                        op0=ALU.mult, op1=ALU.add)
                tsc = E("V", scf, deps=[ev_v[c], t_carry0, t_small])
                t_scan.append(tsc)
                def carf(eb, c=c):
                    return eb.tensor_copy(carry[:, c:c + 1],
                                          vt[:, 512 * c + 511:512 * c + 512])
                E("V", carf, deps=[(tsc[0], tsc[1], tsc[2], True)],
                  tok=False)

            t_h = [None] * FKC
            for g in range(4):
                tok, pks = wide_phase(
                    w1nm, lambda k: xabf[:, csl(k)], ebank, base=KC * g,
                    extra_dep=ev_xa[-1])
                for m in range(8):
                    hm = 8 * g + m
                    def hev(eb, m=m, hm=hm, eix=eix):
                        return eb.activation(
                            hbf[:, csl(hm)], bank(m), AF.Gelu,
                            bias=b1c[:, 32 * eix + hm:32 * eix + hm + 1])
                    tev = E("A", hev, deps=[tok] + hbf_guard)
                    ebank[m] = tev
                    t_h[hm] = tev

            t_y = []
            for c in range(KC):
                def yf(eb, c=c):
                    return eb.tensor_mul(vt[:, csl(c)], qt[:, csl(c)],
                                         vt[:, csl(c)])
                t_y.append(E("V", yf, deps=[t_scan[c], ev_q[c]]))
            t_ybf = []
            t_ysq = []
            for c in range(KC):
                def cbf(eb, c=c):
                    return eb.copy(ybf[:, csl(c)], vt[:, csl(c)])
                t_ybf.append(E("A", cbf, deps=[t_y[c]] + ybf_guard))
                def csq(eb, c=c):
                    return eb.activation(xg[:, csl(c)], vt[:, csl(c)],
                                         AF.Square)
                t_ysq.append(E("A", csq, deps=[t_y[c], t_alast]))

            t_mS3 = E("T", mk_stats(2, ybf),
                      deps=t_ybf + [ebank[2]] + stats_guard)
            t_mQ3 = E("T", mk_stats(3, xg),
                      deps=t_ysq + [ebank[3]] + stats_guard)
            xg_guard = [t_mQ3]

            t_mu3 = E("V", lambda eb: eb.tensor_scalar_mul(
                r_2, bankrow(2), 1.0 / D), deps=[t_mS3])
            ebank[2] = t_mu3
            E("V", lambda eb: eb.tensor_mul(r_1, r_2, r_2), tok=False)
            t_var3 = E("V", lambda eb: eb.scalar_tensor_tensor(
                r_1, bankrow(3), 1.0 / D, r_1, op0=ALU.mult,
                op1=ALU.subtract), deps=[t_mQ3])
            t_ln3 = E("A", lambda eb: eb.activation(r_3, r_1, AF.Ln,
                                                    bias=cEPS),
                      deps=[t_var3, t_rstd2, t_invrr])
            t_rstd3 = E("A", lambda eb: eb.activation(
                r_3, r_3, AF.Exp, bias=0.0, scale=cNH))
            stats_guard = [t_var3]
            ebank[3] = t_var3

            t_w2last, pks = wide_phase(
                w2nm, lambda k: hbf[:, csl(k)], ebank, nk=FKC,
                rhs_deps=t_h)
            hbf_guard = [t_w2last]

            ev_eo = []
            for m in range(8):
                def eoev(eb, m=m, eix=eix):
                    return eb.activation(
                        eobf[:, csl(m)], bank(m), AF.Identity,
                        bias=b2c[:, 8 * eix + m:8 * eix + m + 1])
                tev = E("A", eoev, deps=[t_w2last] + eo_guard)
                ebank[m] = tev
                ev_eo.append(tev)

            t_bm3m = E("T", lambda eb: eb.matmul(bank(2), onef, r_2,
                                                 start=True, stop=True),
                       deps=[t_mu3, t_var3, bc_reads["m3"], ebank[2]])
            t_br3m = E("T", lambda eb: eb.matmul(bank(3), onef, r_3,
                                                 start=True, stop=True),
                       deps=[t_rstd3, bc_reads["r3"], ebank[3]])
            t_bm3 = E("A", lambda eb: eb.copy(bc_m3, bank(2)),
                      deps=[t_bm3m])
            t_br3 = E("A", lambda eb: eb.copy(bc_r3, bank(3)),
                      deps=[t_br3m])
            ebank[2] = t_bm3
            ebank[3] = t_br3

            t_yln = []
            for c in range(KC):
                def l3a(eb, c=c):
                    sl = csl(c)
                    eb.tensor_sub(qt[:, sl], vt[:, sl], bc_m3)
                    eb.tensor_mul(qt[:, sl], qt[:, sl], bc_r3)
                    return eb.tensor_scalar(qt[:, sl], qt[:, sl],
                                            g3c[:, c:c + 1],
                                            b3c[:, c:c + 1], op0=ALU.mult,
                                            op1=ALU.add)
                t_yln.append(E("V", l3a, deps=[t_bm3, t_br3, t_mS3]))
            bc_reads["m3"] = t_yln[-1]
            bc_reads["r3"] = t_yln[-1]

            t_glast, pks = wide_phase(
                "wg", lambda k: xbf[:, csl(k)], ebank,
                krows=[(rsg, negmu)], krow_deps=[t_nmu],
                extra_dep=t_xbf)
            mark_parked("wg", pks)
            t_yg = []
            for m in range(8):
                def gmul(eb, m=m):
                    return eb.tensor_mul(scrv(m % 2), bank(m), bc_rstd)
                t_gm = E("V", gmul, deps=[t_glast, t_bcs, prev_scra[m % 2]])
                def gtnh(eb, m=m):
                    return eb.activation(scra(m % 2), scrv(m % 2), AF.Tanh,
                                         bias=0.0, scale=cPH128)
                t_gs = E("A", gtnh, deps=[t_gm, prev_scra[m % 2]])
                def gu(eb, m=m):
                    return eb.scalar_tensor_tensor(
                        scrv(m % 2), scra(m % 2), 1.0, scrv(m % 2),
                        op0=ALU.add, op1=ALU.mult)
                t_gu = E("V", gu, deps=[t_gs])
                def ygf(eb, m=m):
                    return eb.tensor_mul(ybf[:, csl(m)], qt[:, csl(m)],
                                         scrv(m % 2))
                t_ygm = E("V", ygf, deps=[t_gu, t_yln[m], t_mS3])
                ebank[m] = t_ygm
                prev_scra[m % 2] = t_ygm
                t_yg.append(t_ygm)
            bc_reads["rstd"] = t_yg[-1]
            xbf_guard = [t_glast, t_xg[-1]]

            t_olast, pks = wide_phase(
                "wo", lambda k: ybf[:, csl(k)], ebank,
                rhs_deps=t_yg)
            mark_parked("wo", pks)
            ev_s = []
            for m in range(8):
                def oev(eb, m=m):
                    return eb.activation(stt[:, csl(m)], bank(m),
                                         AF.Identity, bias=0.0,
                                         scale=cQT128)
                tev = E("A", oev, deps=[t_olast] + stt_guard)
                ebank[m] = tev
                ev_s.append(tev)
            ybf_guard = [t_olast]

            t_nxlast, pks = wide_phase(
                "nw", lambda k: eobf[:, csl(k)], ebank,
                rhs_deps=ev_eo)
            mark_parked("nw", pks)
            eo_guard = [t_nxlast]

            t_fin = []
            for m in range(8):
                fs = 2 * (m % 2)
                def fgel(eb, m=m, fs=fs):
                    return eb.activation(scra(fs), bank(m), AF.Gelu,
                                         bias=nbc[:, m:m + 1])
                E("A", fgel, deps=[t_nxlast, prev_scra[m % 2]], tok=False)
                def ftnh(eb, m=m, fs=fs):
                    return eb.activation(scra(fs + 1), scra(fs), AF.Tanh,
                                         bias=0.0, scale=cPH128)
                t_gg = E("A", ftnh)
                ebank[m] = t_gg
                def ffin(eb, m=m, fs=fs):
                    sl = csl(m)
                    eb.scalar_tensor_tensor(
                        scrv(m % 2), scra(fs + 1), 1.0, scra(fs),
                        op0=ALU.add, op1=ALU.mult)
                    eb.tensor_mul(scrv(m % 2), scrv(m % 2), stt[:, sl])
                    return eb.tensor_add(vt[:, sl], scrv(m % 2), xt[:, sl])
                t_fv = E("V", ffin,
                         deps=[t_gg, t_xtl, t_ybf[-1], t_ysq[-1]])
                t_fin.append(t_fv)
                prev_scra[m % 2] = t_fv
            stt_guard = [t_fin[-1]]
            xt_guard = [t_fin[-1], ev_xa[-1]]
            xa_guard = [t_h[-1]]

            def outdma(eb, t0=t0):
                return eb.dma_start(
                    out=outT[:, t0:t0 + T].rearrange("(c p) t -> p c t",
                                                     p=128),
                    in_=vt.rearrange("p (c t) -> p c t", c=KC))
            t_out = E("S", outdma, deps=t_fin, d=16, sem="OU")
            vt_guard = [t_out]
            if ti + 1 < NT:
                pre_x[ti + 1] = (
                    E("G", xload_bf(T * (ti + 1)), deps=xbf_guard, d=16,
                      sem="GX"),
                    E("S", xload_f32(T * (ti + 1)), deps=xt_guard, d=16,
                      sem="XL"))
            row_guard = [t_glast, t_alast, t_b2, t_b3, t_bm3m, t_br3m]

        E("S", lambda eb: eb.engine_nop() if hasattr(eb, "engine_nop")
          else eb.nop(), deps=[t_out], tok=False)

        def emit_for(name):
            def body(eb):
                for (eng, fn, waits, incs) in prog.ops:
                    if eng != name:
                        continue
                    for sname in sorted(waits):
                        eb.wait_ge(sems[sname], waits[sname])
                    ins = fn(eb)
                    for sname, dv in incs:
                        ins.then_inc(sems[sname], dv)
            return body

        block.tensor(emit_for("T"))
        block.vector(emit_for("V"))
        block.scalar(emit_for("A"))
        block.sync(emit_for("S"))
        block.gpsimd(emit_for("G"))

    return nc


_CACHE = {}


def _get_nc():
    if "nc" not in _CACHE:
        _CACHE["nc"] = build_nc()
    return _CACHE["nc"]


def _vec_cols(v):
    return np.ascontiguousarray(v.reshape(8, 128).T)


def kernel(
    L0_attn_x, L0_attn_decay, L0_attn_Wq, L0_attn_Wv, L0_attn_Wo, L0_attn_Wg,
    L0_attn_ln_gamma, L0_attn_ln_beta, L0_ffn_ln_scale, L0_ffn_ln_bias,
    L0_ffn_attn_w, L0_ffn_attn_b, L0_ffn_gate_w, L0_ffn_w1, L0_ffn_b1,
    L0_ffn_w2, L0_ffn_b2, L0_ffn_next_w, L0_ffn_next_b,
):
    f = np.float32
    x = np.asarray(L0_attn_x, f)
    decay = np.asarray(L0_attn_decay, f)
    shared = {
        "wqT": np.ascontiguousarray(np.asarray(L0_attn_Wq, f).T),
        "wvT": np.ascontiguousarray(np.asarray(L0_attn_Wv, f).T),
        "wgT": np.ascontiguousarray(np.asarray(L0_attn_Wg, f).T),
        "woT": np.ascontiguousarray(np.asarray(L0_attn_Wo, f).T),
        "awT": np.ascontiguousarray(np.asarray(L0_ffn_attn_w, f).T),
        "nwT": np.ascontiguousarray(np.asarray(L0_ffn_next_w, f).T),
        "abrow": np.ascontiguousarray(np.asarray(L0_ffn_attn_b, f)[None, :]),
        "nbc": _vec_cols(np.asarray(L0_ffn_next_b, f)),
        "g3c": _vec_cols(np.asarray(L0_attn_ln_gamma, f)),
        "b3c": _vec_cols(np.asarray(L0_attn_ln_beta, f)),
        "g2c": _vec_cols(np.asarray(L0_ffn_ln_scale, f)),
        "b2vc": _vec_cols(np.asarray(L0_ffn_ln_bias, f)),
        "lamh": np.ascontiguousarray(
            decay[(2 * np.arange(8)[None, :] +
                   (np.arange(128) // 64)[:, None])]),
    }
    w1 = np.asarray(L0_ffn_w1, f)
    w2 = np.asarray(L0_ffn_w2, f)
    b1 = np.asarray(L0_ffn_b1, f)
    b2 = np.asarray(L0_ffn_b2, f)
    in_maps = []
    for c in range(NCORES):
        e0 = (c % 4) * 2
        m = dict(shared)
        m["xT"] = np.ascontiguousarray(x[c].T)
        m["w1p"] = np.ascontiguousarray(w1[e0:e0 + 2])
        m["w2p"] = np.ascontiguousarray(w2[e0:e0 + 2])
        m["b1c"] = np.ascontiguousarray(
            np.concatenate([b1[e0 + e].reshape(32, 128).T for e in range(2)],
                           axis=1))
        m["b2c"] = np.ascontiguousarray(
            np.concatenate([b2[e0 + e].reshape(8, 128).T for e in range(2)],
                           axis=1))
        in_maps.append(m)

    nc = _get_nc()
    res = run_bass_kernel_spmd(nc, in_maps, core_ids=list(range(NCORES)))
    out = np.stack([np.asarray(res.results[c]["outT"]).T
                    for c in range(NCORES)])
    return np.ascontiguousarray(out.astype(np.float32))
